# version 01
# baseline (speedup 1.0000x reference)
import numpy as np
import concourse.bass as bass
import concourse.mybir as mybir
from concourse.alu_op_type import AluOpType
from concourse.bass_utils import run_bass_kernel_spmd
from contextlib import ExitStack

F32 = mybir.dt.float32
BF16 = mybir.dt.bfloat16
I32 = mybir.dt.int32
AF = mybir.ActivationFunctionType
AL = AluOpType
AX = mybir.AxisListType.X

S = 8192
D = 1024
ST = 512
NST = S // ST
CH = 128
NCH = S // CH
NH = 8
GQ = 3
NG = 22
NEG = -1.0e9
EPS_LN = 1e-5
LOG_INV32 = float(np.log(1.0 / 32.0))

ENGINES = ("tensor", "vector", "scalar", "gpsimd", "sync")


class KB:

    def __init__(self, nc, se):
        self.nc = nc
        self.se = se
        self.cnt = 0
        self.steps = []

    def step(self, eng, fn, inc=1):
        self.steps.append((eng, fn, inc))

    def dma(self, eng, out, in_, **kw):
        self.step(eng, lambda e, o=out, i=in_, k=kw: e.dma_start(o, i, **k), inc=16)

    def dmat(self, out, in_):
        self.step("sync", lambda e, o=out, i=in_: e.dma_start(o, i, transpose=True), inc=16)

    def flush(self, block):
        plan = []
        for eng, fn, inc in self.steps:
            plan.append((eng, fn, inc, self.cnt))
            self.cnt += inc
        self.steps = []
        se = self.se
        for name in ENGINES:
            sub = [p for p in plan if p[0] == name]
            if not sub:
                continue
            def section(e, sub=sub):
                for _eng, fn, inc, pre in sub:
                    e.wait_ge(se, pre)
                    inst = fn(e)
                    inst.then_inc(se, inc)
            getattr(block, name)(section)

    def final_wait(self, block, eng="gpsimd"):
        tgt = self.cnt
        se = self.se
        getattr(block, eng)(lambda e: e.wait_ge(se, tgt))


def _ln_subtile(k, T, xs, stat, c, tout):
    mean = stat[:, 4 * c:4 * c + 1]
    var = stat[:, 4 * c + 1:4 * c + 2]
    msq = stat[:, 4 * c + 2:4 * c + 3]
    k.step("vector", lambda e: e.reduce_sum(mean, xs, axis=AX))
    k.step("scalar", lambda e: e.activation(T["junk"][:, :], xs, AF.Square, accum_out=var))
    k.step("vector", lambda e: e.tensor_scalar(mean, mean, 1.0 / D, None, op0=AL.mult))
    k.step("vector", lambda e: e.tensor_tensor(msq, mean, mean, op=AL.mult))
    k.step("vector", lambda e: e.scalar_tensor_tensor(var, var, 1.0 / D, msq, op0=AL.mult, op1=AL.subtract))
    k.step("scalar", lambda e: e.activation(var, var, AF.Ln, bias=T["bvals"][:, 0:1], scale=1.0))
    k.step("scalar", lambda e: e.activation(var, var, AF.Exp, bias=0.0, scale=-0.5))
    k.step("vector", lambda e: e.tensor_scalar(tout, xs, mean, var, op0=AL.subtract, op1=AL.mult))


def _leaf_logic(k, sv, g, sc, leaf_out):
    c1, c2, c3 = sc[:, 0:3], sc[:, 3:6], sc[:, 6:9]
    t0, t1 = sc[:, 9:12], sc[:, 12:15]
    u0, u1 = sc[:, 15:18], sc[:, 18:21]
    w0 = sc[:, 21:24]
    k.step("vector", lambda e: e.tensor_scalar(g[0], sv[:, :, 0], 0.0, None, op0=AL.is_ge))
    k.step("vector", lambda e: e.select(c1, g[0], sv[:, :, 2], sv[:, :, 1]))
    k.step("vector", lambda e: e.tensor_scalar(g[1], c1, 0.0, None, op0=AL.is_ge))
    k.step("vector", lambda e: e.select(t0, g[1], sv[:, :, 4], sv[:, :, 3]))
    k.step("vector", lambda e: e.select(t1, g[1], sv[:, :, 6], sv[:, :, 5]))
    k.step("vector", lambda e: e.select(c2, g[0], t1, t0))
    k.step("vector", lambda e: e.tensor_scalar(g[2], c2, 0.0, None, op0=AL.is_ge))
    k.step("vector", lambda e: e.select(t0, g[2], sv[:, :, 8], sv[:, :, 7]))
    k.step("vector", lambda e: e.select(t1, g[2], sv[:, :, 10], sv[:, :, 9]))
    k.step("vector", lambda e: e.select(u0, g[2], sv[:, :, 12], sv[:, :, 11]))
    k.step("vector", lambda e: e.select(u1, g[2], sv[:, :, 14], sv[:, :, 13]))
    k.step("vector", lambda e: e.select(w0, g[1], t1, t0))
    k.step("vector", lambda e: e.select(t1, g[1], u1, u0))
    k.step("vector", lambda e: e.select(c3, g[0], t1, w0))
    k.step("vector", lambda e: e.tensor_scalar(g[3], c3, 0.0, None, op0=AL.is_ge))
    gi = g[4]
    k.step("vector", lambda e: e.scalar_tensor_tensor(gi, g[0], 2, g[1], op0=AL.mult, op1=AL.add))
    k.step("vector", lambda e: e.scalar_tensor_tensor(gi, gi, 2, g[2], op0=AL.mult, op1=AL.add))
    k.step("vector", lambda e: e.scalar_tensor_tensor(gi, gi, 2, g[3], op0=AL.mult, op1=AL.add))
    k.step("vector", lambda e: e.tensor_copy(leaf_out, gi))



def emit_ffn_produce(k, block, T, xsrc, tag):
    nc = k.nc
    with ExitStack() as es:
        xin = es.enter_context(nc.sbuf_tensor(tag + "_xin", [128, 4 * D], F32))
        tbf = es.enter_context(nc.sbuf_tensor(tag + "_t", [128, 4 * D], BF16))
        tT = es.enter_context(nc.sbuf_tensor(tag + "_tT", [128, 8 * ST], BF16))
        stat = es.enter_context(nc.sbuf_tensor(tag + "_stat", [128, 16], F32))
        ssb = es.enter_context(nc.sbuf_tensor(tag + "_ssb", [45, ST], F32))
        stok = es.enter_context(nc.sbuf_tensor(tag + "_stok", [128, 4 * 45], F32))
        msk = es.enter_context(nc.sbuf_tensor(tag + "_msk", [128, 15], I32))
        lsc = es.enter_context(nc.sbuf_tensor(tag + "_lsc", [128, 24], F32))
        leaf = es.enter_context(nc.sbuf_tensor(tag + "_leaf", [128, 4 * 3], F32))
        lfT = es.enter_context(nc.sbuf_tensor(tag + "_lfT", [1, 3 * ST], F32))
        oh16 = es.enter_context(nc.sbuf_tensor(tag + "_oh16", [16, ST], BF16))
        hmask = es.enter_context(nc.sbuf_tensor(tag + "_hmask", [128, 8 * ST], BF16))
        hrelu = es.enter_context(nc.sbuf_tensor(tag + "_hrelu", [128, ST], BF16))
        qko = es.enter_context(nc.sbuf_tensor(tag + "_qko", [128, 8 * ST], BF16))
        vout = es.enter_context(nc.sbuf_tensor(tag + "_vout", [128, 4 * D], BF16))
        sqb = es.enter_context(nc.sbuf_tensor(tag + "_sqb", [128, ST], BF16))
        kmx = es.enter_context(nc.sbuf_tensor(tag + "_kmx", [1, 4], F32))

        pS = es.enter_context(nc.psum_tensor(tag + "_pS", [45, ST], F32))
        pMisc = es.enter_context(nc.psum_tensor(tag + "_pMisc", [128, ST], F32))
        pLR = es.enter_context(nc.psum_tensor(tag + "_pLR", [128, ST], F32))
        pH = es.enter_context(nc.psum_tensor(tag + "_pH", [128, ST], F32))
        pO = [es.enter_context(nc.psum_tensor(f"{tag}_pO{i}", [128, ST], F32)) for i in range(2)]
        pN = es.enter_context(nc.psum_tensor(tag + "_pN", [1, ST], F32))

        xv = xsrc.rearrange("(st sub p) d -> st p sub d", sub=4, p=128)
        qtv = T["QT"].rearrange("(db p) t -> p db t", p=128)
        ktv = T["KT"].rearrange("(db p) t -> p db t", p=128)
        vav = T["VA"].rearrange("(st sub p) d -> st p sub d", sub=4, p=128)

        k.step("vector", lambda e: e.memset(kmx[:, :], 0.0))

        for st in range(NST):
            xi3 = xin.rearrange("p (sub d) -> p sub d", d=D)
            k.dma("sync", xi3[:, :, :], xv[st])
            tb3 = tbf.rearrange("p (sub d) -> p sub d", d=D)
            for sub in range(4):
                _ln_subtile(k, T, xi3[:, sub], stat, sub, tb3[:, sub])
            for ds in range(8):
                for sub in range(4):
                    k.dmat(tT[:, ds * ST + sub * 128: ds * ST + sub * 128 + 128],
                           tbf[:, sub * D + ds * 128: sub * D + ds * 128 + 128])
            for db in range(8):
                k.step("tensor", lambda e, db=db: e.matmul(
                    pS[:, :], T["nodewT"][:, db * 45:(db + 1) * 45],
                    tT[:, db * ST:(db + 1) * ST], start=(db == 0), stop=(db == 7)))
            k.step("vector", lambda e: e.tensor_scalar(ssb[:, :], pS[:, :], T["nodeb"][:, 0:1], None, op0=AL.add))
            for sub in range(4):
                k.step("tensor", lambda e, sub=sub: e.transpose(
                    pMisc[:, sub * 45:(sub + 1) * 45], ssb[:, sub * 128:(sub + 1) * 128], T["identf"][0:45, 0:45]))
            k.step("scalar", lambda e: e.copy(stok[:, :], pMisc[:, 0:180]))
            for sub in range(4):
                sv = stok.rearrange("p (s n i) -> p s n i", s=4, i=15)[:, sub]
                g = [msk[:, 3 * j:3 * j + 3] for j in range(5)]
                _leaf_logic(k, sv, g, lsc, leaf[:, 3 * sub:3 * sub + 3])
            for ni, net in enumerate("qkv"):
                for sub in range(4):
                    k.step("tensor", lambda e, sub=sub, ni=ni: e.transpose(
                        pMisc[0:1, 384:384 + 128], leaf[:, 3 * sub + ni:3 * sub + ni + 1], T["identf"][:, :]))
                    k.step("scalar", lambda e, sub=sub, ni=ni: e.copy(
                        lfT[0:1, ni * ST + sub * 128: ni * ST + (sub + 1) * 128], pMisc[0:1, 384:384 + 128]))
                for sub in range(4):
                    k.step("tensor", lambda e, sub=sub, ni=ni: e.matmul(
                        pLR[:, sub * 128:(sub + 1) * 128], T["onesrowf"][0:1, :],
                        lfT[0:1, ni * ST + sub * 128: ni * ST + (sub + 1) * 128], start=True, stop=True))
                k.step("vector", lambda e: e.tensor_scalar(oh16[:, :], pLR[0:16, :], T["clid16"][:, 0:1], None, op0=AL.is_equal))
                w1 = T["w1_" + net]
                w2 = T["w2_" + net]
                b2 = T["b2_" + net]
                b1c = T["b1_" + net]
                for hs in range(8):
                    for db in range(8):
                        k.step("tensor", lambda e, hs=hs, db=db, w1=w1: e.matmul(
                            pH[:, :], w1[:, db * D + hs * 128: db * D + (hs + 1) * 128],
                            tT[:, db * ST:(db + 1) * ST], start=(db == 0), stop=(db == 7)))
                    k.step("scalar", lambda e, hs=hs, b1c=b1c: e.activation(
                        hrelu[:, :], pH[:, :], AF.Relu, bias=b1c[:, hs:hs + 1], scale=1.0))
                    k.step("vector", lambda e, hs=hs: e.scalar_tensor_tensor(
                        hmask[:, hs * ST:(hs + 1) * ST], pLR[:, :], T["clid"][:, hs:hs + 1],
                        hrelu[:, :], op0=AL.is_equal, op1=AL.mult))
                if net in ("q", "k"):
                    for os in range(8):
                        po = pO[os % 2]
                        for hs in range(8):
                            k.step("tensor", lambda e, os=os, hs=hs, w2=w2, po=po: e.matmul(
                                po[:, :], w2[:, hs * D + os * 128: hs * D + (os + 1) * 128],
                                hmask[:, hs * ST:(hs + 1) * ST], start=(hs == 0), stop=False))
                        k.step("tensor", lambda e, os=os, b2=b2, po=po: e.matmul(
                            po[:, :], b2[0:16, os * 128:(os + 1) * 128], oh16[:, :], start=False, stop=True))
                        k.step("scalar", lambda e, os=os, po=po: e.copy(qko[:, os * ST:(os + 1) * ST], po[:, :]))
                        k.step("vector", lambda e, os=os: e.tensor_tensor(
                            sqb[:, :], qko[:, os * ST:(os + 1) * ST], qko[:, os * ST:(os + 1) * ST], op=AL.mult))
                        k.step("tensor", lambda e, os=os: e.matmul(
                            pN[0:1, :], T["onescolb"][:, 0:1], sqb[:, :], start=(os == 0), stop=(os == 7)))
                    dst = qtv if net == "q" else ktv
                    k.dma("gpsimd", dst[:, :, st * ST:(st + 1) * ST],
                          qko.rearrange("p (db t) -> p db t", t=ST)[:, :, :])
                    if net == "q":
                        k.step("scalar", lambda e, st=st: e.copy(T["negs"][0:1, st * ST:(st + 1) * ST], pN[0:1, :]))
                    else:
                        k.step("vector", lambda e: e.reduce_max(kmx[:, 1:2], pN[0:1, :], axis=AX))
                        k.step("vector", lambda e: e.tensor_tensor(kmx[:, 0:1], kmx[:, 0:1], kmx[:, 1:2], op=AL.max))
                else:
                    for sub in range(4):
                        for half in range(2):
                            po = pO[half]
                            for hs in range(8):
                                k.step("tensor", lambda e, sub=sub, half=half, hs=hs, w2=w2, po=po: e.matmul(
                                    po[:, :], hmask[:, hs * ST + sub * 128: hs * ST + (sub + 1) * 128],
                                    w2[:, hs * D + half * 512: hs * D + (half + 1) * 512],
                                    start=(hs == 0), stop=False))
                            k.step("tensor", lambda e, sub=sub, half=half, b2=b2, po=po: e.matmul(
                                po[:, :], oh16[0:16, sub * 128:(sub + 1) * 128],
                                b2[0:16, half * 512:(half + 1) * 512], start=False, stop=True))
                            k.step("scalar", lambda e, sub=sub, half=half, po=po: e.copy(
                                vout[:, sub * D + half * 512: sub * D + (half + 1) * 512], po[:, :]))
                    k.dma("gpsimd", vav[st], vout.rearrange("p (sub d) -> p sub d", d=D)[:, :, :])
        k.step("vector", lambda e: e.tensor_scalar(T["negs"][0:1, :], T["negs"][0:1, :], kmx[0:1, 0:1], None, op0=AL.mult))
        k.step("scalar", lambda e: e.activation(T["negs"][0:1, :], T["negs"][0:1, :], AF.Ln, bias=0.0, scale=1.0))
        k.step("scalar", lambda e: e.activation(T["negs"][0:1, :], T["negs"][0:1, :], AF.Exp, bias=0.0, scale=0.5))
        k.step("scalar", lambda e: e.activation(T["negs"][0:1, :], T["negs"][0:1, :], AF.Relu, bias=T["bvals"][0:1, 3:4], scale=1.0))
        k.step("vector", lambda e: e.tensor_scalar(T["negs"][0:1, :], T["negs"][0:1, :], -1.0, None, op0=AL.mult))
        k.flush(block)



def emit_ffn_attend(k, block, T, xsrc, xdst, tag):
    nc = k.nc
    with ExitStack() as es:
        vres = es.enter_context(nc.sbuf_tensor(tag + "_vres", [128, NCH * D], BF16))
        ktc = es.enter_context(nc.sbuf_tensor(tag + "_ktc", [128, 2 * 8 * 512], BF16))
        qgrp = es.enter_context(nc.sbuf_tensor(tag + "_qgrp", [128, 8 * GQ * 128], BF16))
        xg = es.enter_context(nc.sbuf_tensor(tag + "_xg", [128, GQ * D], F32))
        px = es.enter_context(nc.sbuf_tensor(tag + "_px", [128, 2 * GQ * 128], BF16))
        xro = es.enter_context(nc.sbuf_tensor(tag + "_xro", [128, GQ * D], F32))
        rc = es.enter_context(nc.sbuf_tensor(tag + "_rc", [128, 4], F32))

        pS2 = es.enter_context(nc.psum_tensor(tag + "_pS2", [128, GQ * 128], F32))
        pY = [es.enter_context(nc.psum_tensor(f"{tag}_pY{i}", [128, D], F32)) for i in range(GQ)]
        pL = es.enter_context(nc.psum_tensor(tag + "_pL", [128, 4], F32))

        vload = T["VA"].rearrange("(grp c p) d -> grp p c d", c=4, p=128)
        vr3 = vres.rearrange("p (c d) -> p c d", d=D)
        qtv = T["QT"].rearrange("(db p) t -> p db t", p=128)
        ktv = T["KT"].rearrange("(db p) t -> p db t", p=128)
        xsv = xsrc.rearrange("(t p) d -> t p d", p=128)
        xdv = xdst.rearrange("(t p) d -> t p d", p=128)

        for i in range(16):
            k.dma("sync", vr3[:, 4 * i:4 * i + 4, :], vload[i])

        for g in range(NG):
            gq = GQ if g < NG - 1 else NCH - GQ * (NG - 1)
            qc = gq * 128
            q3 = qgrp.rearrange("p (db t) -> p db t", t=GQ * 128)
            k.dma("sync", q3[:, :, 0:qc], qtv[:, :, g * GQ * 128: g * GQ * 128 + qc])
            for t in range(gq):
                k.dma("sync", xg.rearrange("p (t d) -> p t d", d=D)[:, t], xsv[g * GQ + t])
            for c in range(NCH):
                if c % 4 == 0:
                    kc = ktc[:, (c // 4 % 2) * 8 * 512:(c // 4 % 2 + 1) * 8 * 512]
                    k.dma("sync", kc.rearrange("p (db t) -> p db t", t=512)[:, :, :],
                          ktv[:, :, c * 128: c * 128 + 512])
                kc = ktc[:, (c // 4 % 2) * 8 * 512:(c // 4 % 2 + 1) * 8 * 512]
                k.step("tensor", lambda e, g=g, qc=qc: e.matmul(
                    pS2[:, 0:qc], T["onesrowb"][0:1, :],
                    T["negs"][0:1, g * GQ * 128: g * GQ * 128 + qc], start=True, stop=False))
                for db in range(8):
                    k.step("tensor", lambda e, db=db, c=c, qc=qc, kc=kc: e.matmul(
                        pS2[:, 0:qc], kc[:, db * 512 + (c % 4) * 128: db * 512 + (c % 4) * 128 + 128],
                        qgrp.rearrange("p (db t) -> p db t", t=GQ * 128)[:, db, 0:qc],
                        start=False, stop=(db == 7)))
                pxc = px[:, (c % 2) * GQ * 128:(c % 2) * GQ * 128 + qc]
                k.step("scalar", lambda e, pxc=pxc, qc=qc: e.activation(pxc, pS2[:, 0:qc], AF.Exp, bias=0.0, scale=1.0 / 32.0))
                for t in range(gq):
                    lh = px[:, (c % 2) * GQ * 128 + t * 128:(c % 2) * GQ * 128 + (t + 1) * 128]
                    for half in range(2):
                        k.step("tensor", lambda e, t=t, c=c, half=half, lh=lh: e.matmul(
                            pY[t][:, half * 512:(half + 1) * 512], lh,
                            vr3[:, c, half * 512:(half + 1) * 512], start=(c == 0), stop=(c == NCH - 1)))
                    k.step("tensor", lambda e, t=t, c=c, lh=lh: e.matmul(
                        pL[:, t:t + 1], lh, T["onescolb"][:, 0:1],
                        start=(c == 0 and t == 0), stop=(c == NCH - 1)))
            k.step("vector", lambda e, gq=gq: e.reciprocal(rc[:, 0:gq], pL[:, 0:gq]))
            k.step("vector", lambda e, gq=gq: e.tensor_scalar(rc[:, 0:gq], rc[:, 0:gq], 0.5, None, op0=AL.mult))
            for t in range(gq):
                k.step("vector", lambda e, t=t: e.scalar_tensor_tensor(
                    xro.rearrange("p (t d) -> p t d", d=D)[:, t], pY[t][:, :], rc[:, t:t + 1],
                    xg.rearrange("p (t d) -> p t d", d=D)[:, t], op0=AL.mult, op1=AL.add))
                k.dma("gpsimd", xdv[g * GQ + t], xro.rearrange("p (t d) -> p t d", d=D)[:, t])
        k.flush(block)



def emit_ssm(k, block, T, xsrc, xdst):
    nc = k.nc
    with ExitStack() as es:
        wqT = es.enter_context(nc.sbuf_tensor("b_wqT", [128, 8 * 512], BF16))
        wkT = es.enter_context(nc.sbuf_tensor("b_wkT", [128, 8 * 512], BF16))
        waT = es.enter_context(nc.sbuf_tensor("b_waT", [128, 8 * 512], BF16))
        wab = es.enter_context(nc.sbuf_tensor("b_wab", [128, 4], F32))
        wv = es.enter_context(nc.sbuf_tensor("b_wv", [128, 8 * D], BF16))
        woT = es.enter_context(nc.sbuf_tensor("b_woT", [128, 8 * D], BF16))
        vtr = es.enter_context(nc.sbuf_tensor("b_vtr", [128, 8 * D], BF16))
        for name, t in (("wqT", wqT), ("wkT", wkT), ("walT", waT)):
            k.dma("gpsimd", t.rearrange("p (db o) -> p db o", o=512)[:, :, :],
                  T[name].rearrange("(db p) o -> p db o", p=128))
        k.dma("sync", wab[:, :], T["wabc"])
        k.dma("gpsimd", wv.rearrange("p (m d) -> p m d", d=D)[:, :, :],
              T["wvn"].rearrange("(m p) d -> p m d", p=128))
        k.dma("gpsimd", woT.rearrange("p (m o) -> p m o", o=D)[:, :, :],
              T["woT"].rearrange("(m p) o -> p m o", p=128))

        T["maskadd"] = es.enter_context(nc.sbuf_tensor("b_maskadd", [128, 128], F32))
        T["uones"] = es.enter_context(nc.sbuf_tensor("b_uones", [128, 128], F32))
        T["const8"] = es.enter_context(nc.sbuf_tensor("b_const8", [8, D], F32))
        T["i8"] = es.enter_context(nc.sbuf_tensor("b_i8", [8, 8], F32))
        T["hmb"] = es.enter_context(nc.sbuf_tensor("b_hmb", [128, 32], F32))
        T["hma"] = es.enter_context(nc.sbuf_tensor("b_hma", [128, 32], F32))
        for nm, dst in (("maskadd_h", "maskadd"), ("uones_h", "uones"), ("const8_h", "const8"),
                        ("i8_h", "i8"), ("hmb_h", "hmb"), ("hma_h", "hma")):
            k.dma("sync", T[dst][:, :], k.d_in[nm][:, :])
        atok = es.enter_context(nc.sbuf_tensor("b_atok", [128, NCH * 8], F32))
        btok = es.enter_context(nc.sbuf_tensor("b_btok", [128, NCH * 8], F32))

        xv = xsrc.rearrange("(st sub p) d -> st p sub d", sub=4, p=128)
        vsv = T["VS"].rearrange("(st sub p) d -> st p sub d", sub=4, p=128)

        with ExitStack() as e1:
            xin = e1.enter_context(nc.sbuf_tensor("b_xin", [128, 4 * D], F32))
            tbf = e1.enter_context(nc.sbuf_tensor("b_t", [128, 4 * D], BF16))
            tT = e1.enter_context(nc.sbuf_tensor("b_tT", [128, 8 * ST], BF16))
            stat = e1.enter_context(nc.sbuf_tensor("b_stat", [128, 16], F32))
            qTs = e1.enter_context(nc.sbuf_tensor("b_qTs", [128, 4 * 512], F32))
            esb = e1.enter_context(nc.sbuf_tensor("b_esb", [128, 4 * 512], F32))
            spsb = e1.enter_context(nc.sbuf_tensor("b_spsb", [128, 4 * 512], F32))
            sp1 = e1.enter_context(nc.sbuf_tensor("b_sp1", [128, 512], F32))
            sp2 = e1.enter_context(nc.sbuf_tensor("b_sp2", [128, 512], F32))
            babs = e1.enter_context(nc.sbuf_tensor("b_babs", [8, 512], F32))
            vssb = e1.enter_context(nc.sbuf_tensor("b_vssb", [128, 4 * D], BF16))

            pP = [e1.enter_context(nc.psum_tensor(f"b_pP{i}", [128, 512], F32)) for i in range(2)]
            pB = e1.enter_context(nc.psum_tensor("b_pB", [8, 512], F32))
            pA = e1.enter_context(nc.psum_tensor("b_pA", [8, 512], F32))
            pT8 = e1.enter_context(nc.psum_tensor("b_pT8", [128, 32], F32))

            for ds in range(8):
                for half in range(2):
                    po = pP[half]
                    for mb in range(8):
                        k.step("tensor", lambda e, ds=ds, half=half, mb=mb, po=po: e.matmul(
                            po[:, :], wv[:, mb * D + ds * 128: mb * D + (ds + 1) * 128],
                            woT[:, mb * D + half * 512: mb * D + (half + 1) * 512],
                            start=(mb == 0), stop=(mb == 7)))
                    k.step("scalar", lambda e, ds=ds, half=half, po=po: e.copy(
                        vtr[:, ds * D + half * 512: ds * D + (half + 1) * 512], po[:, :]))

            for st in range(NST):
                xi3 = xin.rearrange("p (sub d) -> p sub d", d=D)
                k.dma("sync", xi3[:, :, :], xv[st])
                tb3 = tbf.rearrange("p (sub d) -> p sub d", d=D)
                for sub in range(4):
                    _ln_subtile(k, T, xi3[:, sub], stat, sub, tb3[:, sub])
                for ds in range(8):
                    for sub in range(4):
                        k.dmat(tT[:, ds * ST + sub * 128: ds * ST + sub * 128 + 128],
                               tbf[:, sub * D + ds * 128: sub * D + ds * 128 + 128])
                for os in range(12):
                    w = (wqT, wkT, waT)[os // 4]
                    sl = os % 4
                    po = pP[os % 2]
                    for db in range(8):
                        k.step("tensor", lambda e, db=db, sl=sl, w=w, po=po: e.matmul(
                            po[:, :], w[:, db * 512 + sl * 128: db * 512 + (sl + 1) * 128],
                            tT[:, db * ST:(db + 1) * ST], start=(db == 0), stop=(db == 7)))
                    if os < 4:
                        k.step("scalar", lambda e, sl=sl, po=po: e.copy(qTs[:, sl * 512:(sl + 1) * 512], po[:, :]))
                    elif os < 8:
                        k.step("vector", lambda e, sl=sl, po=po: e.tensor_tensor(
                            esb[:, sl * 512:(sl + 1) * 512], qTs[:, sl * 512:(sl + 1) * 512], po[:, :], op=AL.mult))
                    else:
                        k.step("vector", lambda e, sl=sl, po=po: e.tensor_scalar(
                            sp1[:, :], po[:, :], wab[:, sl:sl + 1], None, op0=AL.add))
                        k.step("scalar", lambda e: e.activation(sp2[:, :], sp1[:, :], AF.Abs))
                        k.step("scalar", lambda e: e.activation(sp2[:, :], sp2[:, :], AF.Exp, bias=0.0, scale=-1.0))
                        k.step("scalar", lambda e: e.activation(sp2[:, :], sp2[:, :], AF.Ln, bias=T["bvals"][:, 1:2], scale=1.0))
                        k.step("vector", lambda e: e.tensor_scalar(sp1[:, :], sp1[:, :], 0.0, None, op0=AL.max))
                        k.step("vector", lambda e, sl=sl: e.tensor_tensor(
                            spsb[:, sl * 512:(sl + 1) * 512], sp1[:, :], sp2[:, :], op=AL.add))
                for sl in range(4):
                    k.step("tensor", lambda e, sl=sl: e.matmul(
                        pB[:, :], T["hmb"][:, sl * 8:(sl + 1) * 8], esb[:, sl * 512:(sl + 1) * 512],
                        start=(sl == 0), stop=(sl == 3)))
                for sl in range(4):
                    k.step("tensor", lambda e, sl=sl: e.matmul(
                        pA[:, :], T["hma"][:, sl * 8:(sl + 1) * 8], spsb[:, sl * 512:(sl + 1) * 512],
                        start=(sl == 0), stop=(sl == 3)))
                for psrc, dst in ((pB, btok), (pA, atok)):
                    k.step("scalar", lambda e, psrc=psrc: e.copy(babs[:, :], psrc[0:8, :]))
                    for j in range(4):
                        k.step("tensor", lambda e, j=j: e.transpose(
                            pT8[:, 0:8], babs[0:8, 128 * j:128 * (j + 1)], T["identf"][0:8, 0:8]))
                        k.step("scalar", lambda e, j=j, dst=dst, st=st: e.copy(
                            dst[:, (st * 4 + j) * 8:(st * 4 + j) * 8 + 8], pT8[:, 0:8]))
                for sub in range(4):
                    for half in range(2):
                        po = pP[half]
                        for db in range(8):
                            k.step("tensor", lambda e, sub=sub, half=half, db=db, po=po: e.matmul(
                                po[:, :], tT[:, db * ST + sub * 128: db * ST + (sub + 1) * 128],
                                vtr[:, db * D + half * 512: db * D + (half + 1) * 512],
                                start=(db == 0), stop=(db == 7)))
                        k.step("scalar", lambda e, sub=sub, half=half, po=po: e.copy(
                            vssb[:, sub * D + half * 512: sub * D + (half + 1) * 512], po[:, :]))
                k.dma("gpsimd", vsv[st], vssb.rearrange("p (sub d) -> p sub d", d=D)[:, :, :])
            k.flush(block)

        if "ab" in T.get("dbgmap", {}):
            k.dma("gpsimd", T["dbgmap"]["ab"][:, 0:NCH * 8], atok[:, :])
            k.dma("gpsimd", T["dbgmap"]["ab"][:, NCH * 8:2 * NCH * 8], btok[:, :])
        with ExitStack() as e2:
            vsc = e2.enter_context(nc.sbuf_tensor("b2_vsc", [128, 2 * D], BF16))
            xrc = e2.enter_context(nc.sbuf_tensor("b2_xrc", [128, 2 * D], F32))
            xout = e2.enter_context(nc.sbuf_tensor("b2_xout", [128, 2 * D], F32))
            cTs = e2.enter_context(nc.sbuf_tensor("b2_cTs", [8, 128], F32))
            ctok = e2.enter_context(nc.sbuf_tensor("b2_ctok", [128, 8], F32))
            cntok = e2.enter_context(nc.sbuf_tensor("b2_cntok", [128, 8], F32))
            cnTs = e2.enter_context(nc.sbuf_tensor("b2_cnTs", [8, 128], F32))
            ngg = e2.enter_context(nc.sbuf_tensor("b2_ngg", [128, 8], F32))
            eTs = e2.enter_context(nc.sbuf_tensor("b2_eTs", [8, 128], F32))
            enT = e2.enter_context(nc.sbuf_tensor("b2_enT", [8, 128], BF16))
            ehm = e2.enter_context(nc.sbuf_tensor("b2_ehm", [8, 128], BF16))
            Msb = e2.enter_context(nc.sbuf_tensor("b2_Msb", [128, 128], F32))
            WT = e2.enter_context(nc.sbuf_tensor("b2_WT", [128, 8 * 128], BF16))
            WnT = e2.enter_context(nc.sbuf_tensor("b2_WnT", [128, 8 * 128], BF16))
            ztot = e2.enter_context(nc.sbuf_tensor("b2_ztot", [128, 8], F32))
            lnz = e2.enter_context(nc.sbuf_tensor("b2_lnz", [128, 8], F32))
            diag = e2.enter_context(nc.sbuf_tensor("b2_diag", [8, 8], F32))
            u0f = e2.enter_context(nc.sbuf_tensor("b2_u0f", [8, 2 * D], F32))
            u0b = e2.enter_context(nc.sbuf_tensor("b2_u0b", [8, 2 * D], BF16))
            z0c = e2.enter_context(nc.sbuf_tensor("b2_z0c", [8, 2], F32))

            pAux = e2.enter_context(nc.psum_tensor("b2_pAux", [128, 512], F32))
            pTT = e2.enter_context(nc.psum_tensor("b2_pTT", [128, 512], F32))
            pW = [e2.enter_context(nc.psum_tensor(f"b2_pW{i}", [128, 512], F32)) for i in range(2)]
            pY2 = e2.enter_context(nc.psum_tensor("b2_pY2", [128, D], F32))
            pSt = e2.enter_context(nc.psum_tensor("b2_pSt", [8, D], F32))

            vsr = T["VS"].rearrange("(c p) d -> c p d", p=128)
            xrv = xsrc.rearrange("(c p) d -> c p d", p=128)
            xdv = xdst.rearrange("(c p) d -> c p d", p=128)

            k.step("gpsimd", lambda e: e.memset(u0f[:, :], 0.0))
            k.step("gpsimd", lambda e: e.memset(u0b[:, :], 0.0))
            k.step("gpsimd", lambda e: e.memset(z0c[:, :], 0.0))

            for c in range(NCH):
                cc, nn = c % 2, (c + 1) % 2
                k.dma("sync", vsc[:, cc * D:(cc + 1) * D], vsr[c])
                k.dma("sync", xrc[:, cc * D:(cc + 1) * D], xrv[c])
                k.step("tensor", lambda e, c=c: e.matmul(
                    pAux[0:8, 128:256], atok[:, c * 8: c * 8 + 8], T["uones"][:, :], start=True, stop=True))
                k.step("scalar", lambda e: e.copy(cTs[:, :], pAux[0:8, 128:256]))
                k.step("tensor", lambda e: e.transpose(pTT[:, 0:8], cTs[:, :], T["identf"][0:8, 0:8]))
                k.step("scalar", lambda e: e.copy(ctok[:, :], pTT[:, 0:8]))
                k.step("vector", lambda e, c=c: e.tensor_tensor(
                    ngg[:, :], btok[:, c * 8: c * 8 + 8], ctok[:, :], op=AL.subtract))
                k.step("scalar", lambda e: e.activation(eTs[:, :], cTs[:, :], AF.Exp))
                for h in range(NH):
                    reg = pW[h // 4][:, (h % 4) * 128:(h % 4 + 1) * 128]
                    k.step("tensor", lambda e, h=h, reg=reg: e.matmul(
                        reg, T["const8"][:, h * 128:(h + 1) * 128], cTs[:, :],
                        start=(h % 4 == 0), stop=(h % 4 == 3)))
                    k.step("vector", lambda e, reg=reg: e.tensor_tensor(Msb[:, :], reg, T["maskadd"][:, :], op=AL.add))
                    k.step("scalar", lambda e, h=h: e.activation(
                        WT[:, h * 128:(h + 1) * 128], Msb[:, :], AF.Exp, bias=ngg[:, h:h + 1], scale=1.0))
                    k.step("tensor", lambda e, h=h: e.matmul(
                        pAux[:, h:h + 1], WT[:, h * 128:(h + 1) * 128], T["onescolb"][:, 0:1],
                        start=(h == 0), stop=False))
                k.step("vector", lambda e, cc=cc: e.tensor_scalar(
                    diag[:, :], T["i8"][:, :], z0c[:, cc:cc + 1], None, op0=AL.mult))
                k.step("tensor", lambda e: e.matmul(
                    pAux[:, 0:8], eTs[:, :], diag[:, :], start=False, stop=True))
                k.step("scalar", lambda e: e.copy(ztot[:, :], pAux[:, 0:8]))
                k.step("scalar", lambda e: e.activation(lnz[:, :], ztot[:, :], AF.Ln))
                k.step("vector", lambda e: e.tensor_tensor(cntok[:, :], ctok[:, :], lnz[:, :], op=AL.subtract))
                k.step("tensor", lambda e: e.transpose(pTT[0:8, 128:256], cntok[:, :], T["identf"][:, :]))
                k.step("scalar", lambda e: e.copy(cnTs[:, :], pTT[0:8, 128:256]))
                k.step("scalar", lambda e: e.activation(enT[:, :], cnTs[:, :], AF.Exp))
                for h in range(NH):
                    reg = pW[h // 4][:, (h % 4) * 128:(h % 4 + 1) * 128]
                    k.step("tensor", lambda e, h=h, reg=reg: e.matmul(
                        reg, T["const8"][:, h * 128:(h + 1) * 128], cnTs[:, :],
                        start=(h % 4 == 0), stop=(h % 4 == 3)))
                    k.step("vector", lambda e, reg=reg: e.tensor_tensor(Msb[:, :], reg, T["maskadd"][:, :], op=AL.add))
                    k.step("scalar", lambda e, h=h: e.activation(
                        WnT[:, h * 128:(h + 1) * 128], Msb[:, :], AF.Exp, bias=ngg[:, h:h + 1], scale=1.0))
                    for half in range(2):
                        k.step("tensor", lambda e, h=h, half=half, cc=cc: e.matmul(
                            pY2[:, half * 512:(half + 1) * 512], WnT[:, h * 128:(h + 1) * 128],
                            vsc[:, cc * D + half * 512: cc * D + (half + 1) * 512],
                            start=(h == 0), stop=False))
                    k.step("vector", lambda e, h=h: e.tensor_scalar(
                        ehm[:, :], enT[:, :], T["i8"][:, h:h + 1], None, op0=AL.mult))
                    for half in range(2):
                        k.step("tensor", lambda e, h=h, half=half, cc=cc: e.matmul(
                            pY2[:, half * 512:(half + 1) * 512], ehm[:, :],
                            u0b[:, cc * D + half * 512: cc * D + (half + 1) * 512],
                            start=False, stop=(h == NH - 1 and half == 1)))
                if c == 0 and "b2i" in T.get("dbgmap", {}):
                    dbgt = T["dbgmap"]["b2i"]
                    k.dma("gpsimd", dbgt[:, 0:1024], WT[:, :])
                    k.dma("gpsimd", dbgt[:, 1024:2048], WnT[:, :])
                    k.dma("gpsimd", dbgt[:, 2048:2056], ztot[:, :])
                    k.dma("gpsimd", dbgt[:, 2056:2064], ngg[:, :])
                    k.dma("gpsimd", dbgt[:, 2064:2072], ctok[:, :])
                    k.dma("gpsimd", dbgt[0:8, 2072:2200], cTs[:, :])
                    k.dma("gpsimd", dbgt[0:8, 2200:2328], eTs[:, :])
                wlast = WT.rearrange("p (h j) -> p h j", j=128)[:, :, 127]
                for half in range(2):
                    k.step("tensor", lambda e, half=half, cc=cc, wlast=wlast: e.matmul(
                        pSt[:, half * 512:(half + 1) * 512], wlast,
                        vsc[:, cc * D + half * 512: cc * D + (half + 1) * 512], start=True, stop=True))
                k.step("tensor", lambda e, wlast=wlast: e.matmul(
                    pAux[0:8, 16:17], wlast, T["onescolb"][:, 0:1], start=True, stop=True))
                k.step("vector", lambda e, cc=cc, nn=nn: e.scalar_tensor_tensor(
                    u0f[:, nn * D:(nn + 1) * D], u0f[:, cc * D:(cc + 1) * D],
                    eTs[:, 127:128], pSt[:, :], op0=AL.mult, op1=AL.add))
                k.step("vector", lambda e, cc=cc, nn=nn: e.scalar_tensor_tensor(
                    z0c[:, nn:nn + 1], z0c[:, cc:cc + 1], eTs[:, 127:128], pAux[0:8, 16:17],
                    op0=AL.mult, op1=AL.add))
                k.step("scalar", lambda e, nn=nn: e.copy(u0b[:, nn * D:(nn + 1) * D], u0f[:, nn * D:(nn + 1) * D]))
                k.step("vector", lambda e, cc=cc: e.scalar_tensor_tensor(
                    xout[:, cc * D:(cc + 1) * D], pY2[:, :], 1.0,
                    xrc[:, cc * D:(cc + 1) * D], op0=AL.mult, op1=AL.add))
                k.dma("gpsimd", xdv[c], xout[:, cc * D:(cc + 1) * D])
            k.flush(block)



def build_nc(dbg=(), phases="ABC"):
    nc = bass.Bass("TRN2", target_bir_lowering=False)
    T = {}
    d_in = {}
    names_f32 = [
        ("x", [S, D]), ("nodewT_h", [D, 45]), ("nodeb_h", [45, 1]),
        ("w1q_h", [D, D]), ("w1k_h", [D, D]), ("w1v_h", [D, D]),
        ("b1q_h", [128, 8]), ("b1k_h", [128, 8]), ("b1v_h", [128, 8]),
        ("w2q_h", [D, D]), ("w2k_h", [D, D]), ("w2v_h", [D, D]),
        ("b2q_h", [16, D]), ("b2k_h", [16, D]), ("b2v_h", [16, D]),
        ("wqT_h", [D, 512]), ("wkT_h", [D, 512]), ("walT_h", [D, 512]),
        ("wab_h", [128, 4]), ("wv_h", [D, D]), ("woT_h", [D, D]),
        ("ident_h", [128, 128]), ("maskadd_h", [128, 128]), ("uones_h", [128, 128]),
        ("const8_h", [8, D]), ("i8_h", [8, 8]), ("hmb_h", [128, 32]), ("hma_h", [128, 32]),
        ("clid_h", [128, 8]), ("clid16_h", [16, 1]), ("onescol_h", [128, 1]), ("onesrow_h", [1, 128]), ("bvals_h", [128, 4]),
    ]
    for n, shp in names_f32:
        d_in[n] = nc.dram_tensor(n, shp, F32, kind="ExternalInput").ap()
    out = nc.dram_tensor("out", [S, D], F32, kind="ExternalOutput").ap()
    T["QT"] = nc.dram_tensor("QTs", [D, S], BF16, kind="Internal").ap()
    T["KT"] = nc.dram_tensor("KTs", [D, S], BF16, kind="Internal").ap()
    T["VA"] = nc.dram_tensor("VAs", [S, D], BF16, kind="Internal").ap()
    T["VS"] = nc.dram_tensor("VSs", [S, D], BF16, kind="Internal").ap()
    T["XR"] = nc.dram_tensor("XRs", [S, D], F32, kind="Internal").ap()
    T["XR2"] = nc.dram_tensor("XR2s", [S, D], F32, kind="Internal").ap()
    T["wvn"] = d_in["wv_h"]
    T["woT"] = d_in["woT_h"]
    T["wqT"] = d_in["wqT_h"]
    T["wkT"] = d_in["wkT_h"]
    T["walT"] = d_in["walT_h"]
    T["wabc"] = d_in["wab_h"]
    dbg_out = {}
    for name in dbg:
        shp, dt = {"qt": ([D, S], BF16), "kt": ([D, S], BF16), "va": ([S, D], BF16),
                   "vs": ([S, D], BF16), "xr": ([S, D], F32), "xr2": ([S, D], F32),
                   "negs": ([1, S], BF16), "ab": ([128, 2 * NCH * 8], F32), "b2i": ([128, 2328], F32)}[name]
        dbg_out[name] = nc.dram_tensor("dbg_" + name, shp, dt, kind="ExternalOutput").ap()

    with ExitStack() as perm, nc.semaphore("SE") as se, nc.Block() as block:
        k = KB(nc, se)
        k.d_in = d_in
        T["identb"] = perm.enter_context(nc.sbuf_tensor("c_identb", [128, 128], BF16))
        T["identf"] = perm.enter_context(nc.sbuf_tensor("c_identf", [128, 128], F32))
        T["clid"] = perm.enter_context(nc.sbuf_tensor("c_clid", [128, 8], F32))
        T["clid16"] = perm.enter_context(nc.sbuf_tensor("c_clid16", [16, 1], F32))
        T["onescolb"] = perm.enter_context(nc.sbuf_tensor("c_onescolb", [128, 1], BF16))
        T["onesrowb"] = perm.enter_context(nc.sbuf_tensor("c_onesrowb", [1, 128], BF16))
        T["onesrowf"] = perm.enter_context(nc.sbuf_tensor("c_onesrowf", [1, 128], F32))
        T["negs"] = perm.enter_context(nc.sbuf_tensor("c_negs", [1, S], BF16))
        T["junk"] = perm.enter_context(nc.sbuf_tensor("c_junk", [128, D], BF16))
        T["bvals"] = perm.enter_context(nc.sbuf_tensor("c_bvals", [128, 4], F32))

        for nm, dst in (("ident_h", "identf"),
                        ("clid_h", "clid"), ("clid16_h", "clid16"), ("onesrow_h", "onesrowf"), ("bvals_h", "bvals")):
            k.dma("sync", T[dst][:, :], d_in[nm][:, :])
        for nm, dst in (("ident_h", "identb"), ("onescol_h", "onescolb"), ("onesrow_h", "onesrowb")):
            k.dma("gpsimd", T[dst][:, :], d_in[nm][:, :])

        def load_ffn_weights(es, tag):
            T["nodewT"] = es.enter_context(nc.sbuf_tensor(tag + "_nodewT", [128, 8 * 45], BF16))
            T["nodeb"] = es.enter_context(nc.sbuf_tensor(tag + "_nodeb", [45, 1], F32))
            k.dma("gpsimd", T["nodewT"].rearrange("p (db n) -> p db n", n=45)[:, :, :],
                  d_in["nodewT_h"].rearrange("(db p) n -> p db n", p=128))
            k.dma("sync", T["nodeb"][:, :], d_in["nodeb_h"][:, :])
            for net in "qkv":
                for pre in ("w1", "w2"):
                    t = es.enter_context(nc.sbuf_tensor(f"{tag}_{pre}{net}", [128, 8 * D], BF16))
                    T[f"{pre}_{net}"] = t
                    k.dma("gpsimd", t.rearrange("p (db c) -> p db c", c=D)[:, :, :],
                          d_in[f"{pre}{net}_h"].rearrange("(db p) c -> p db c", p=128))
                t = es.enter_context(nc.sbuf_tensor(f"{tag}_b1{net}", [128, 8], F32))
                T[f"b1_{net}"] = t
                k.dma("sync", t[:, :], d_in[f"b1{net}_h"][:, :])
                t = es.enter_context(nc.sbuf_tensor(f"{tag}_b2{net}", [16, D], BF16))
                T[f"b2_{net}"] = t
                k.dma("gpsimd", t[:, :], d_in[f"b2{net}_h"][:, :])

        if "A" in phases or "a" in phases:
            with ExitStack() as esA:
                load_ffn_weights(esA, "wa")
                emit_ffn_produce(k, block, T, d_in["x"], "pa")
            if "qt" in dbg_out:
                k.dma("gpsimd", dbg_out["qt"][:, :], T["QT"][:, :])
            if "kt" in dbg_out:
                k.dma("gpsimd", dbg_out["kt"][:, :], T["KT"][:, :])
            if "va" in dbg_out:
                k.dma("gpsimd", dbg_out["va"][:, :], T["VA"][:, :])
            if "negs" in dbg_out:
                k.dma("gpsimd", dbg_out["negs"][0:1, :], T["negs"][0:1, :])
        if "A" in phases:
            emit_ffn_attend(k, block, T, d_in["x"], T["XR"], "fa")
            if "xr" in dbg_out:
                k.dma("gpsimd", dbg_out["xr"][:, :], T["XR"][:, :])
        k.flush(block)
        T["dbgmap"] = dbg_out
        if "B" in phases:
            emit_ssm(k, block, T, T["XR"], T["XR2"])
            if "vs" in dbg_out:
                k.dma("gpsimd", dbg_out["vs"][:, :], T["VS"][:, :])
            if "xr2" in dbg_out:
                k.dma("gpsimd", dbg_out["xr2"][:, :], T["XR2"][:, :])
            k.flush(block)
        if "C" in phases:
            with ExitStack() as esC:
                load_ffn_weights(esC, "wc")
                emit_ffn_produce(k, block, T, T["XR2"], "pc")
            emit_ffn_attend(k, block, T, T["XR2"], out, "fc")
            k.flush(block)
        k.final_wait(block, "gpsimd")
    return nc



def _host_consts():
    c = {}
    idx = np.arange(128)
    c["ident_h"] = np.eye(128, dtype=np.float32)
    c["maskadd_h"] = np.where(idx[:, None] > idx[None, :], np.float32(NEG), np.float32(0.0)).astype(np.float32)
    c["uones_h"] = (idx[:, None] <= idx[None, :]).astype(np.float32)
    c8 = np.zeros((8, D), np.float32)
    for h in range(8):
        c8[h, h * 128:(h + 1) * 128] = 1.0
    c["const8_h"] = c8
    c["i8_h"] = np.eye(8, dtype=np.float32)
    hmb = np.zeros((128, 32), np.float32)
    hma = np.zeros((128, 32), np.float32)
    for sl in range(4):
        for p in range(128):
            h = (128 * sl + p) // 64
            hmb[p, sl * 8 + h] = 0.125
            hma[p, sl * 8 + h] = -1.0
    c["hmb_h"] = hmb
    c["hma_h"] = hma
    clid = np.zeros((128, 8), np.float32)
    for hs in range(8):
        clid[:, hs] = (hs * 128 + idx) // 64
    c["clid_h"] = clid
    c["clid16_h"] = np.arange(16, dtype=np.float32).reshape(16, 1)
    c["onescol_h"] = np.ones((128, 1), np.float32)
    c["onesrow_h"] = np.ones((1, 128), np.float32)
    bv = np.zeros((128, 4), np.float32)
    bv[:, 0] = EPS_LN; bv[:, 1] = 1.0; bv[:, 2] = LOG_INV32; bv[:, 3] = -1440.0
    c["bvals_h"] = bv
    return c


def _host_weights(inputs):
    w = {}
    A = lambda x: np.ascontiguousarray(np.asarray(x), dtype=np.float32)
    w["nodewT_h"] = A(np.concatenate([np.asarray(inputs[f"L0_ffn_{n}_node_w"]).T for n in "qkv"], axis=1))
    w["nodeb_h"] = A(np.concatenate([np.asarray(inputs[f"L0_ffn_{n}_node_b"]) for n in "qkv"]).reshape(45, 1))
    for n in "qkv":
        w[f"w1{n}_h"] = A(np.asarray(inputs[f"L0_ffn_{n}_w1"]).transpose(1, 0, 2).reshape(D, D))
        w[f"b1{n}_h"] = A(np.asarray(inputs[f"L0_ffn_{n}_b1"]).reshape(D).reshape(8, 128).T)
        w[f"w2{n}_h"] = A(np.asarray(inputs[f"L0_ffn_{n}_w2"]).reshape(D, D))
        w[f"b2{n}_h"] = A(inputs[f"L0_ffn_{n}_b2"])
    w["wqT_h"] = A(np.asarray(inputs["L0_attn_W_q"]).T)
    w["wkT_h"] = A(np.asarray(inputs["L0_attn_W_k"]).T)
    w["walT_h"] = A(np.asarray(inputs["L0_attn_W_alpha_w"]).T)
    w["wab_h"] = A(np.asarray(inputs["L0_attn_W_alpha_b"]).reshape(4, 128).T)
    w["wv_h"] = A(inputs["L0_attn_W_v"])
    w["woT_h"] = A(np.asarray(inputs["L0_attn_W_o"]).T)
    return w


def kernel(**inputs):
    x = np.ascontiguousarray(np.asarray(inputs["L0_attn_x"]), dtype=np.float32)
    shared = {}
    shared.update(_host_consts())
    shared.update(_host_weights(inputs))
    nc = build_nc()
    in_maps = [dict(shared, x=np.ascontiguousarray(x[i])) for i in range(8)]
    res = run_bass_kernel_spmd(nc, in_maps, core_ids=list(range(8)))
    return np.concatenate([np.asarray(r["out"], dtype=np.float32)[None] for r in res.results], axis=0)
